# version 07
# baseline (speedup 1.0000x reference)
import numpy as np
import concourse.bass as bass
import concourse.mybir as mybir

F32 = mybir.dt.float32
AF = mybir.ActivationFunctionType
ALU = mybir.AluOpType
AX = mybir.AxisListType
NEG = -1e30

D = 1024
HID = 2048
NE = 4
WIN = 128
CH = 512
NMT = CH // WIN
S = 4096
EPS = 1e-5
EPSG = 1e-8
NSLOPE = 0.01
INV_SQRT_H = 1.0 / 32.0

REG = 8192
RA, RB, RC, RD = 0, REG, 2 * REG, 3 * REG


def build_nc(n_chunks=8, debug=None):
    nc = bass.Bass("TRN2", target_bir_lowering=False)

    x_d = nc.dram_tensor("x", [S, D], F32, kind="ExternalInput").ap()
    wqT_d = nc.dram_tensor("wqT", [D, D], F32, kind="ExternalInput").ap()
    wkT_d = nc.dram_tensor("wkT", [D, D], F32, kind="ExternalInput").ap()
    wvT_d = nc.dram_tensor("wvT", [D, D], F32, kind="ExternalInput").ap()
    bq_d = nc.dram_tensor("bq", [128, 8], F32, kind="ExternalInput").ap()
    bk_d = nc.dram_tensor("bk", [128, 8], F32, kind="ExternalInput").ap()
    bv_d = nc.dram_tensor("bv", [1, D], F32, kind="ExternalInput").ap()
    gam_d = nc.dram_tensor("gam", [1, D], F32, kind="ExternalInput").ap()
    bet_d = nc.dram_tensor("bet", [1, D], F32, kind="ExternalInput").ap()
    rwT_d = nc.dram_tensor("rwT", [D, NE], F32, kind="ExternalInput").ap()
    rng_d = nc.dram_tensor("rng", [128, 8], F32, kind="ExternalInput").ap()
    rnb_d = nc.dram_tensor("rnb", [128, 8], F32, kind="ExternalInput").ap()
    w1_d = nc.dram_tensor("w1", [NE, D, HID], F32, kind="ExternalInput").ap()
    lng_d = nc.dram_tensor("lng", [128, NE * 16], F32, kind="ExternalInput").ap()
    lnb_d = nc.dram_tensor("lnb", [128, NE * 16], F32, kind="ExternalInput").ap()
    w2_d = nc.dram_tensor("w2", [NE, HID, D], F32, kind="ExternalInput").ap()
    ong_d = nc.dram_tensor("ong", [1, D], F32, kind="ExternalInput").ap()
    onb_d = nc.dram_tensor("onb", [1, D], F32, kind="ExternalInput").ap()
    ident_d = nc.dram_tensor("ident", [128, 128], F32, kind="ExternalInput").ap()
    y_d = nc.dram_tensor("y", [S, D], F32, kind="ExternalOutput").ap()
    if debug:
        dbgH_d = nc.dram_tensor("dbgH", [128, NMT * D], F32, kind="ExternalOutput").ap()
        dbgC_d = nc.dram_tensor("dbgC", [128, NE * NMT], F32, kind="ExternalOutput").ap()
        dbgA_d = nc.dram_tensor("dbgA", [128, NMT * D], F32, kind="ExternalOutput").ap()

    SLOT = nc.alloc_sbuf_tensor("SLOT", [128, 4 * REG], F32).ap()
    H = nc.alloc_sbuf_tensor("H", [128, NMT * D], F32).ap()
    ACC = nc.alloc_sbuf_tensor("ACC", [128, NMT * D], F32).ap()
    X = nc.alloc_sbuf_tensor("X", [128, 2 * D], F32).ap()
    QF = nc.alloc_sbuf_tensor("QF", [128, D], F32).ap()
    KF = nc.alloc_sbuf_tensor("KF", [128, D], F32).ap()
    VT = nc.alloc_sbuf_tensor("VT", [128, D], F32).ap()
    M8 = nc.alloc_sbuf_tensor("M8", [128, 8], F32).ap()
    M8B = nc.alloc_sbuf_tensor("M8B", [128, 8], F32).ap()
    HIDN = nc.alloc_sbuf_tensor("HIDN", [128, HID], F32).ap()
    HFR = nc.alloc_sbuf_tensor("HFR", [128, 8 * 128], F32).ap()
    HDR = nc.alloc_sbuf_tensor("HDR", [128, 2 * 128], F32).ap()
    SC = HFR[:, 256:384]
    EA = HFR[:, 384:512]
    PP = HFR[:, 512:640]
    PT = HFR[:, 640:768]
    ZA = HDR[:, 0:128]
    ZB = HDR[:, 128:256]
    BPAR = nc.alloc_sbuf_tensor("BPAR", [128, 2 * D], F32).ap()
    BVB = nc.alloc_sbuf_tensor("BVB", [128, D], F32).ap()
    L1F = HIDN[:, 0:D]
    L1 = HIDN[:, D:2 * D]
    U = VT
    FN = QF
    OUT = KF

    BQ = nc.alloc_sbuf_tensor("BQ", [128, 8], F32).ap()
    BK = nc.alloc_sbuf_tensor("BK", [128, 8], F32).ap()
    RNG = nc.alloc_sbuf_tensor("RNG", [128, 8], F32).ap()
    RNB = nc.alloc_sbuf_tensor("RNB", [128, 8], F32).ap()
    LNG = nc.alloc_sbuf_tensor("LNG", [128, NE * 16], F32).ap()
    LNB = nc.alloc_sbuf_tensor("LNB", [128, NE * 16], F32).ap()
    RWT = nc.alloc_sbuf_tensor("RWT", [128, 8 * NE], F32).ap()
    GT = nc.alloc_sbuf_tensor("GT", [128, 8 * NE], F32).ap()
    TB = nc.alloc_sbuf_tensor("TB", [128, 8 * NE], F32).ap()
    GB4 = nc.alloc_sbuf_tensor("GB4", [128, NE], F32).ap()
    CB4 = nc.alloc_sbuf_tensor("CB4", [128, NE], F32).ap()
    GC1 = nc.alloc_sbuf_tensor("GC1", [1, 2 * NE], F32).ap()
    ROWD = QF[0:1, 0:D]
    ONES_R = nc.alloc_sbuf_tensor("ONES_R", [1, 128], F32).ap()
    ONES_C = nc.alloc_sbuf_tensor("ONES_C", [128, 1], F32).ap()
    IDT = nc.alloc_sbuf_tensor("IDT", [128, 128], F32).ap()
    CONST = nc.alloc_sbuf_tensor("CONST", [128, 4], F32).ap()
    MUSIG = nc.alloc_sbuf_tensor("MUSIG", [128, 2 * NMT], F32).ap()
    C4 = nc.alloc_sbuf_tensor("C4", [128, NE * NMT], F32).ap()
    ST = nc.alloc_sbuf_tensor("ST", [128, 64], F32).ap()
    S4 = nc.alloc_sbuf_tensor("S4", [128, 8 * NE], F32).ap()


    PA = nc.alloc_psum_tensor("PA", [128, 1024], F32).ap()
    PB = nc.alloc_psum_tensor("PB", [128, 1024], F32).ap()
    PC = nc.alloc_psum_tensor("PC", [128, 1024], F32).ap()
    PD = nc.alloc_psum_tensor("PD", [128, 512], F32).ap()
    PE = nc.alloc_psum_tensor("PE", [128, 512], F32).ap()

    q = {"sync": [], "tensor": [], "vector": [], "scalar": [], "gpsimd": []}
    cnt = {"T": 0, "V": 0, "Sg": 0}

    import contextlib
    stack = contextlib.ExitStack()
    sT = stack.enter_context(nc.semaphore("sT"))
    sV = stack.enter_context(nc.semaphore("sV"))
    sS = stack.enter_context(nc.semaphore("sSg"))
    SEM = {"T": sT, "V": sV, "Sg": sS}
    LDP = [stack.enter_context(nc.semaphore(f"sL{i}")) for i in range(8)]
    XP = [stack.enter_context(nc.semaphore(f"sX{i}")) for i in range(2)]
    OP = [stack.enter_context(nc.semaphore(f"sO{i}")) for i in range(2)]
    dq = {
        "ld": {"q": "sync", "sems": LDP, "tot": [0] * len(LDP), "g": 0},
        "x0": {"q": "gpsimd", "sems": XP, "tot": [0] * len(XP), "g": 0},
        "st": {"q": "gpsimd", "sems": OP, "tot": [0] * len(OP), "g": 0},
    }

    def emit(eng, fn):
        q[eng].append(fn)

    def inc(name, k=1):
        cnt[name] += k
        return cnt[name]

    def w(eng, name, val):
        assert val <= cnt[name], (eng, name, val, cnt[name])
        emit(eng, lambda e, s=SEM[name], v=val: e.wait_ge(s, v))

    def E(eng, fn, inc_sem=None, inck=1):
        if eng == "vector" and inc_sem is None:
            inc_sem = "V"
        if inc_sem is None:
            emit(eng, fn)
            return None
        v = inc(inc_sem, inck)
        emit(eng, lambda e, f=fn, s=SEM[inc_sem], k=inck: f(e).then_inc(s, k))
        if eng == "vector":
            w("vector", "V", cnt["V"])
        return v

    def vbar(fn):
        v = E("vector", fn, "V")
        w("vector", "V", v)
        return v

    def dma_group(pool, pairs):
        d = dq[pool]
        i = d["g"] % len(d["sems"])
        d["g"] += 1
        s = d["sems"][i]
        prev = d["tot"][i]
        if prev:
            emit(d["q"], lambda e, s_=s, v=prev: e.wait_ge(s_, v))
        for o, inp in pairs:
            d["tot"][i] += 16
            emit(d["q"], lambda e, o_=o, i_=inp, s_=s:
                 e.dma_start(out=o_, in_=i_).then_inc(s_, 16))
        return (pool, i, d["tot"][i])

    def wdma(eng, tok):
        pool, i, v = tok
        emit(eng, lambda e, s=dq[pool]["sems"][i], v_=v: e.wait_ge(s, v_))

    def wdma_all(eng, pool):
        d = dq[pool]
        for i, s in enumerate(d["sems"]):
            if d["tot"][i]:
                emit(eng, lambda e, s_=s, v=d["tot"][i]: e.wait_ge(s_, v))

    c_eps = CONST[:, 0:1]
    c_i32 = CONST[:, 1:2]

    stats_raw = ST[:, 0:24]
    st_mu = ST[:, 26:27]
    st_var = ST[:, 27:28]
    st_sstd = ST[:, 28:29]
    st_rstd = ST[:, 29:30]
    st_negmur = ST[:, 30:31]

    def layer_stats(src_ap, width, psum=False, mu_out=None, rstd_out=None,
                    sstd_out=None):
        mu_o = mu_out if mu_out is not None else st_mu
        sstd_o = sstd_out if sstd_out is not None else st_sstd
        rstd_o = rstd_out if rstd_out is not None else st_rstd
        nch = (width + 511) // 512
        for i in range(nch):
            c0 = i * 512
            c1 = min(width, c0 + 512)
            vv = E("vector", lambda e, s=src_ap[:, c0:c1], o=stats_raw[:, i * 6:(i + 1) * 6]:
                   e.bn_stats(o, s), "V")
        w("vector", "V", cnt["V"])
        vbar(lambda e, i=nch: e.bn_aggr(ST[:, 24:26], stats_raw[:, 0:6 * i]))
        vbar(lambda e, o=mu_o: e.tensor_copy(o, ST[:, 24:25]))
        vv = E("vector", lambda e: e.tensor_copy(st_var, ST[:, 25:26]), "V")
        w("vector", "V", vv)
        w("scalar", "V", vv)
        sv = E("scalar", lambda e, o=sstd_o: e.activation(
            out=o, in_=st_var, func=AF.Sqrt, bias=c_eps, scale=1.0), "Sg")
        w("vector", "Sg", sv)
        vbar(lambda e, o=rstd_o, i=sstd_o: e.reciprocal(o, i))
        vbar(lambda e, m=mu_o, r=rstd_o: e.tensor_scalar(
            st_negmur, m, r, -1.0, op0=ALU.mult, op1=ALU.mult))

    const_pairs = [(BQ, bq_d), (BK, bk_d), (RNG, rng_d), (RNB, rnb_d),
                   (LNG, lng_d), (LNB, lnb_d), (IDT, ident_d), (ROWD, bv_d)]
    for kc in range(8):
        const_pairs.append((RWT[:, kc * NE:(kc + 1) * NE],
                            rwT_d[kc * 128:(kc + 1) * 128, :]))
    tok_const = dma_group("ld", const_pairs)

    E("vector", lambda e: e.memset(CONST[:, 0:1], EPS))
    E("vector", lambda e: e.memset(CONST[:, 1:2], INV_SQRT_H))
    E("vector", lambda e: e.memset(ONES_R, 1.0))
    E("vector", lambda e: e.memset(ONES_C, 1.0))
    wdma("vector", tok_const)
    for kc in range(8):
        E("vector", lambda e, k=kc: e.tensor_scalar(
            GT[:, k * NE:(k + 1) * NE], RWT[:, k * NE:(k + 1) * NE],
            RNG[:, k:k + 1], None, op0=ALU.mult))
        E("vector", lambda e, k=kc: e.tensor_scalar(
            TB[:, k * NE:(k + 1) * NE], RWT[:, k * NE:(k + 1) * NE],
            RNB[:, k:k + 1], None, op0=ALU.mult))
    vlast = E("vector", lambda e: e.engine_nop(), "V")
    w("tensor", "V", vlast)
    for kc in range(8):
        tv = E("tensor", lambda e, k=kc: e.matmul(
            PE[0:1, 0:NE], ONES_C, GT[:, k * NE:(k + 1) * NE],
            start=(k == 0), stop=(k == 7)), "T" if kc == 7 else None)
    w("vector", "T", tv)
    E("vector", lambda e: e.tensor_copy(GC1[:, 0:NE], PE[0:1, 0:NE]), "V")
    vv = E("vector", lambda e: e.engine_nop(), "V")
    w("tensor", "V", vv)
    for kc in range(8):
        tv = E("tensor", lambda e, k=kc: e.matmul(
            PE[0:1, NE:2 * NE], ONES_C, TB[:, k * NE:(k + 1) * NE],
            start=(k == 0), stop=(k == 7)), "T" if kc == 7 else None)
    w("vector", "T", tv)
    vv = E("vector", lambda e: e.tensor_copy(GC1[:, NE:2 * NE], PE[0:1, NE:2 * NE]), "V")
    w("tensor", "V", vv)
    tv = E("tensor", lambda e: e.matmul(PE[:, 0:2 * NE], ONES_R, GC1,
                                        start=True, stop=True), "T")
    w("vector", "T", tv)
    E("vector", lambda e: e.tensor_copy(GB4, PE[:, 0:NE]))
    vv = E("vector", lambda e: e.tensor_copy(CB4, PE[:, NE:2 * NE]), "V")
    w("tensor", "V", vv)

    def bcast_row(dst_ap, row_dram, tag):
        w("sync", "T", cnt["T"])
        w("sync", "V", cnt["V"])
        tok = dma_group("ld", [(ROWD, row_dram)])
        wdma("tensor", tok)
        for hh in range(2):
            tv = E("tensor", lambda e, h=hh: e.matmul(
                PE, ONES_R, ROWD[:, h * 512:(h + 1) * 512],
                start=True, stop=True), "T")
            w("vector", "T", tv)
            vv = E("vector", lambda e, h=hh, d=dst_ap: e.tensor_copy(
                d[:, h * 512:(h + 1) * 512], PE), "V")
            w("tensor", "V", vv)
        return vv

    for hh in range(2):
        tv = E("tensor", lambda e, h=hh: e.matmul(
            PE, ONES_R, ROWD[:, h * 512:(h + 1) * 512], start=True, stop=True), "T")
        w("vector", "T", tv)
        vv = E("vector", lambda e, h=hh: e.tensor_copy(
            BVB[:, h * 512:(h + 1) * 512], PE), "V")
        w("tensor", "V", vv)
    bcast_row(BPAR[:, 0:D], gam_d, "gam")
    last_bpar_v = bcast_row(BPAR[:, D:2 * D], bet_d, "bet")

    def wq_base(c):
        return RA if c % 2 == 0 else RD

    def w1p1_base(c):
        return RD if c % 2 == 0 else RA

    ld_done = {}

    def load_wq(c):
        base = wq_base(c)
        ld_done[("wq", c)] = dma_group("ld", [
            (SLOT[:, base + kc * D: base + (kc + 1) * D],
             wqT_d[kc * 128:(kc + 1) * 128, :]) for kc in range(8)])

    def load_wkv(c):
        ld_done[("wk", c)] = dma_group("ld", [
            (SLOT[:, RB + kc * D: RB + (kc + 1) * D],
             wkT_d[kc * 128:(kc + 1) * 128, :]) for kc in range(8)])
        ld_done[("wv", c)] = dma_group("ld", [
            (SLOT[:, RC + kc * D: RC + (kc + 1) * D],
             wvT_d[kc * 128:(kc + 1) * 128, :]) for kc in range(8)])

    def load_w1p1(c, e):
        base = w1p1_base(c)
        ld_done[("w1p1", c, e)] = dma_group("ld", [
            (SLOT[:, base + kc * HID: base + (kc + 1) * HID],
             w1_d[e, kc * 128:(kc + 1) * 128, :]) for kc in range(4)])

    def load_w1p2(c, e):
        base = wq_base(c)
        ld_done[("w1p2", c, e)] = dma_group("ld", [
            (SLOT[:, base + kc * HID: base + (kc + 1) * HID],
             w1_d[e, (4 + kc) * 128:(5 + kc) * 128, :]) for kc in range(4)])

    def load_w2(c, e):
        ld_done[("w2", c, e)] = dma_group("ld", [
            (SLOT[:, RB + hc * D: RB + (hc + 1) * D],
             w2_d[e, hc * 128:(hc + 1) * 128, :]) for hc in range(16)])

    def wq_ap(c, kc, oc):
        base = wq_base(c)
        return SLOT[:, base + kc * D + oc * 128: base + kc * D + (oc + 1) * 128]

    def wk_ap(c, kc, oc):
        return SLOT[:, RB + kc * D + oc * 128: RB + kc * D + (oc + 1) * 128]

    def wv_ap(c, kc, half):
        return SLOT[:, RC + kc * D + half * 512: RC + kc * D + (half + 1) * 512]

    def w1_ap(c, kc, nq):
        base = w1p1_base(c) if kc < 4 else wq_base(c)
        k = kc if kc < 4 else kc - 4
        return SLOT[:, base + k * HID + nq * 512: base + k * HID + (nq + 1) * 512]

    def w2_ap(c, hc, half):
        return SLOT[:, RB + hc * D + half * 512: RB + hc * D + (half + 1) * 512]

    rel = {}

    load_wq(0)
    load_wkv(0)
    load_w1p1(0, 0)

    for c in range(n_chunks):
        wdma_all("scalar", "st")
        arel = {}

        def st_ln1(i):
            xb = X[:, (i % 2) * D:(i % 2 + 1) * D]
            key = ("xfree", c, i - 2) if i >= 2 else ("xfree", c - 1, i + 2)
            if key in rel:
                w("gpsimd", "V", rel[key])
            r0 = c * CH + i * WIN
            tok_x0 = dma_group("x0", [(xb, x_d[r0:r0 + 128, :])])
            wdma("vector", tok_x0)
            layer_stats(xb, D)
            arel[("L1", i)] = E("vector", lambda e, x=xb: e.tensor_scalar(
                L1, x, st_rstd, st_negmur, op0=ALU.mult, op1=ALU.add), "V")

        def st_ln1tr(i):
            w("tensor", "V", arel[("L1", i)])
            for rnd in range(2):
                if rnd == 1:
                    w("tensor", "Sg", cnt["Sg"])
                for j in range(4):
                    kc = rnd * 4 + j
                    tv = E("tensor", lambda e, k=kc, j_=j: e.transpose(
                        PD[:, j_ * 128:(j_ + 1) * 128],
                        L1[:, k * 128:(k + 1) * 128], IDT), "T")
                w("scalar", "T", tv)
                E("scalar", lambda e, r=rnd: e.copy(
                    L1F[:, r * 512:(r + 1) * 512], PD), "Sg")
            w("tensor", "Sg", cnt["Sg"])

        def st_qk(i):
            if i == 0 and ("onfree", c - 1) in rel:
                w("tensor", "V", rel[("onfree", c - 1)])
            wdma("tensor", ld_done[("wq", c)])
            t_q0 = t_q1 = None
            for oc in range(8):
                for kc in range(8):
                    wap = wq_ap(c, kc, oc)
                    tv = E("tensor", lambda e, o=oc, k=kc, a=wap: e.matmul(
                        PA[:, o * 128:(o + 1) * 128], a,
                        L1F[:, k * 128:(k + 1) * 128],
                        start=(k == 0), stop=(k == 7)),
                        "T" if (oc in (3, 7) and kc == 7) else None)
                    if oc == 3 and kc == 7:
                        t_q0 = tv
                    if oc == 7 and kc == 7:
                        t_q1 = tv
            if i == NMT - 1:
                rel[("wq_free", c)] = t_q1
            w("scalar", "T", t_q0)
            for oc in range(4):
                E("scalar", lambda e, o=oc: e.add(
                    out=QF[:, o * 128:(o + 1) * 128],
                    in_=PA[:, o * 128:(o + 1) * 128], add=BQ[:, o:o + 1]), "Sg")
            w("scalar", "T", t_q1)
            for oc in range(4, 8):
                E("scalar", lambda e, o=oc: e.add(
                    out=QF[:, o * 128:(o + 1) * 128],
                    in_=PA[:, o * 128:(o + 1) * 128], add=BQ[:, o:o + 1]), "Sg")
            wdma("tensor", ld_done[("wk", c)])
            t_k0 = t_k1 = None
            for oc in range(8):
                for kc in range(8):
                    wap = wk_ap(c, kc, oc)
                    tv = E("tensor", lambda e, o=oc, k=kc, a=wap: e.matmul(
                        PB[:, o * 128:(o + 1) * 128], a,
                        L1F[:, k * 128:(k + 1) * 128],
                        start=(k == 0), stop=(k == 7)),
                        "T" if (oc in (3, 7) and kc == 7) else None)
                    if oc == 3 and kc == 7:
                        t_k0 = tv
                    if oc == 7 and kc == 7:
                        t_k1 = tv
            if i == NMT - 1:
                rel[("wk_free", c)] = t_k1
            w("scalar", "T", t_k0)
            for oc in range(4):
                E("scalar", lambda e, o=oc: e.add(
                    out=KF[:, o * 128:(o + 1) * 128],
                    in_=PB[:, o * 128:(o + 1) * 128], add=BK[:, o:o + 1]), "Sg")
            w("scalar", "T", t_k1)
            sv = None
            for oc in range(4, 8):
                sv = E("scalar", lambda e, o=oc: e.add(
                    out=KF[:, o * 128:(o + 1) * 128],
                    in_=PB[:, o * 128:(o + 1) * 128], add=BK[:, o:o + 1]), "Sg")
            arel[("kev", i)] = sv

        def st_scores(i):
            w("tensor", "Sg", arel[("kev", i)])
            key = ("scfree", c, i - 1) if i >= 1 else ("scfree", c - 1, NMT - 1)
            if key in rel:
                w("tensor", "V", rel[key])
            for kc in range(8):
                tv = E("tensor", lambda e, k=kc: e.matmul(
                    PE[:, 0:128], QF[:, k * 128:(k + 1) * 128],
                    KF[:, k * 128:(k + 1) * 128],
                    start=(k == 0), stop=(k == 7)), "T" if kc == 7 else None)
            arel[("tsc", i)] = tv

        def st_v(i):
            wdma("tensor", ld_done[("wv", c)])
            tv = None
            for half in range(2):
                for kc in range(8):
                    wap = wv_ap(c, kc, half)
                    tv = E("tensor", lambda e, h=half, k=kc, a=wap: e.matmul(
                        PC[:, h * 512:(h + 1) * 512],
                        L1F[:, k * 128:(k + 1) * 128], a,
                        start=(k == 0), stop=(k == 7)),
                        "T" if kc == 7 else None)
            arel[("tv", i)] = tv
            if i == NMT - 1:
                rel[("wv_free", c)] = tv

        def st_softmax(i):
            w("vector", "T", arel[("tsc", i)])
            rel[("scfree", c, i)] = E(
                "vector", lambda e: e.tensor_copy(SC, PE[:, 0:128]), "V")
            vbar(lambda e: e.max(out=M8, in_=SC))
            vnm = vbar(lambda e: e.tensor_scalar(
                ST[:, 31:32], M8[:, 0:1], -INV_SQRT_H, None, op0=ALU.mult))
            vbar(lambda e: e.match_replace(out=ZA, in_to_replace=M8, in_values=SC,
                                           imm_value=NEG))
            vbar(lambda e: e.max(out=M8, in_=ZA))
            vbar(lambda e: e.match_replace(out=ZB, in_to_replace=M8, in_values=ZA,
                                           imm_value=NEG))
            vbar(lambda e: e.max(out=M8, in_=ZB))
            vbar(lambda e: e.match_replace(out=ZA, in_to_replace=M8, in_values=ZB,
                                           imm_value=NEG))
            vbar(lambda e: e.max(out=M8, in_=ZA))
            vbar(lambda e: e.match_replace(out=ZB, in_to_replace=M8, in_values=ZA,
                                           imm_value=NEG))
            E("vector", lambda e: e.tensor_scalar(
                ZA, ZB, NEG, None, op0=ALU.is_equal), "V")
            w("scalar", "V", vnm)
            sv = E("scalar", lambda e: e.activation(
                out=EA, in_=SC, func=AF.Exp, bias=ST[:, 31:32], scale=c_i32), "Sg")
            w("vector", "Sg", sv)
            key = ("x1", c, i - 1) if i >= 1 else ("x1", c - 1, NMT - 1)
            if key in rel:
                w("scalar", "V", rel[key])
            w("scalar", "T", arel[("tv", i)])
            E("scalar", lambda e: e.copy(VT, PC), "Sg")
            E("vector", lambda e: e.tensor_mul(EA, EA, ZA))
            vbar(lambda e: e.reduce_sum(ST[:, 31:32], EA, axis=AX.X))
            vbar(lambda e: e.reciprocal(ST[:, 30:31], ST[:, 31:32]))
            arel[("pp", i)] = E("vector", lambda e: e.tensor_scalar(
                PP, EA, ST[:, 30:31], None, op0=ALU.mult), "V")

        def st_ptav(i):
            w("tensor", "V", arel[("pp", i)])
            tv = E("tensor", lambda e: e.transpose(PD[:, 0:128], PP, IDT), "T")
            w("scalar", "T", tv)
            sv = E("scalar", lambda e: e.copy(PT, PD[:, 0:128]), "Sg")
            w("tensor", "Sg", sv)
            for half in range(2):
                tv = E("tensor", lambda e, h=half: e.matmul(
                    PC[:, h * 512:(h + 1) * 512], PT,
                    VT[:, h * 512:(h + 1) * 512], start=True, stop=True),
                    "T" if half == 1 else None)
            arel[("tav", i)] = tv

        def st_epiA(i):
            w("vector", "T", arel[("tav", i)])
            E("vector", lambda e: e.tensor_add(U, PC, BVB))
            arel[("leak", i)] = E("vector", lambda e: e.scalar_tensor_tensor(
                U, U, NSLOPE, U, op0=ALU.mult, op1=ALU.max), "V")
            E("vector", lambda e: e.tensor_add(U, U, L1))

        def st_epiB(i):
            xb = X[:, (i % 2) * D:(i % 2 + 1) * D]
            layer_stats(U, D)
            E("vector", lambda e: e.tensor_scalar(
                U, U, st_rstd, st_negmur, op0=ALU.mult, op1=ALU.add))
            E("vector", lambda e: e.tensor_mul(U, U, BPAR[:, 0:D]))
            E("vector", lambda e: e.tensor_add(U, U, BPAR[:, D:2 * D]))
            rel[("x1", c, i)] = E("vector", lambda e, x=xb: e.tensor_add(x, x, U),
                                  "V")
            layer_stats(xb, D, mu_out=MUSIG[:, 2 * i:2 * i + 1],
                        sstd_out=MUSIG[:, 2 * i + 1:2 * i + 2])
            vv = E("vector", lambda e, x=xb, m=i: e.tensor_scalar(
                H[:, m * D:(m + 1) * D], x, st_rstd, st_negmur,
                op0=ALU.mult, op1=ALU.add), "V")
            rel[("xfree", c, i)] = vv
            arel[("H", i)] = vv

        def st_router_T(i):
            w("tensor", "V", arel[("H", i)])
            w("tensor", "V", arel[("leak", i)])
            key = ("s4free", c, i - 1) if i >= 1 else ("s4free", c - 1, NMT - 1)
            if key in rel:
                w("tensor", "V", rel[key])
            ev_r = {}

            def r_tr(k):
                s_ = k % 2
                pdst = PD[:, 0:128] if s_ == 0 else PC[:, 0:128]
                tv = E("tensor", lambda e, k_=k, p=pdst: e.transpose(
                    p, H[:, i * D + k_ * 128: i * D + (k_ + 1) * 128], IDT), "T")
                w("scalar", "T", tv)
                ev_r[k] = E("scalar", lambda e, ss=s_, p=pdst: e.copy(
                    HFR[:, ss * 128:(ss + 1) * 128], p), "Sg")

            r_tr(0)
            for kc in range(8):
                if kc + 1 < 8:
                    r_tr(kc + 1)
                w("tensor", "Sg", ev_r[kc])
                tv = E("tensor", lambda e, k=kc, s_=kc % 2: e.matmul(
                    PE[:, 128:128 + NE], HFR[:, s_ * 128:(s_ + 1) * 128],
                    GT[:, k * NE:(k + 1) * NE],
                    start=(k == 0), stop=(k == 7)), "T" if kc == 7 else None)
            arel[("tsr", i)] = tv

        def st_router_V(i):
            layer_stats(H[:, i * D:(i + 1) * D], D)
            vbar(lambda e: e.tensor_scalar(
                ST[:, 33:34], st_mu, -1.0, None, op0=ALU.mult))
            vbar(lambda e: e.tensor_copy(ST[:, 34:35], st_rstd))
            w("vector", "T", arel[("tsr", i)])
            rel[("s4free", c, i)] = vbar(
                lambda e: e.tensor_copy(S4[:, 0:NE], PE[:, 128:128 + NE]))
            vbar(lambda e: e.scalar_tensor_tensor(
                S4[:, NE:2 * NE], GB4, ST[:, 33:34], S4[:, 0:NE],
                op0=ALU.mult, op1=ALU.add))
            vbar(lambda e: e.scalar_tensor_tensor(
                S4[:, NE:2 * NE], S4[:, NE:2 * NE], ST[:, 34:35], CB4,
                op0=ALU.mult, op1=ALU.add))
            vbar(lambda e: e.reduce_max(ST[:, 35:36], S4[:, NE:2 * NE], axis=AX.X))
            vneg = vbar(lambda e: e.tensor_scalar(
                ST[:, 35:36], ST[:, 35:36], -1.0, None, op0=ALU.mult))
            w("scalar", "V", vneg)
            sv = E("scalar", lambda e: e.activation(
                out=S4[:, 2 * NE:3 * NE], in_=S4[:, NE:2 * NE], func=AF.Exp,
                bias=ST[:, 35:36], scale=1.0, accum_out=ST[:, 36:37]), "Sg")
            w("vector", "Sg", sv)
            vbar(lambda e: e.reciprocal(ST[:, 37:38], ST[:, 36:37]))
            vbar(lambda e: e.tensor_scalar(
                S4[:, 2 * NE:3 * NE], S4[:, 2 * NE:3 * NE], ST[:, 37:38], None,
                op0=ALU.mult))
            vbar(lambda e: e.memset(S4[:, 3 * NE:4 * NE], NEG))
            vbar(lambda e: e.max(out=M8B, in_=S4[:, 2 * NE:4 * NE]))
            vbar(lambda e: e.tensor_tensor(
                ST[:, 42:43], M8B[:, 0:1], M8B[:, 1:2], op=ALU.add))
            vbar(lambda e: e.tensor_scalar(
                ST[:, 42:43], ST[:, 42:43], EPSG, None, op0=ALU.add))
            vbar(lambda e: e.reciprocal(ST[:, 43:44], ST[:, 42:43]))
            vbar(lambda e: e.tensor_scalar(
                S4[:, 3 * NE:4 * NE], S4[:, 2 * NE:3 * NE], M8B[:, 1:2], None,
                op0=ALU.is_ge))
            vbar(lambda e: e.tensor_scalar(
                S4[:, 2 * NE:3 * NE], S4[:, 2 * NE:3 * NE], ST[:, 43:44], None,
                op0=ALU.mult))
            E("vector", lambda e, m=i: e.tensor_mul(
                C4[:, m * NE:(m + 1) * NE], S4[:, 2 * NE:3 * NE],
                S4[:, 3 * NE:4 * NE]), "V")

        st_ln1(0)
        for i in range(NMT):
            st_ln1tr(i)
            st_qk(i)
            st_scores(i)
            if i >= 1:
                st_router_T(i - 1)
            st_v(i)
            st_softmax(i)
            st_ptav(i)
            st_epiA(i)
            if i + 1 < NMT:
                st_ln1(i + 1)
            if i >= 1:
                st_router_V(i - 1)
            st_epiB(i)
        st_router_T(NMT - 1)
        st_router_V(NMT - 1)

        w("sync", "T", rel[("wq_free", c)])
        load_w1p2(c, 0)
        w("sync", "T", rel[("wv_free", c)])
        load_w2(c, 0)

        def hfr_fill(m):
            for rnd in range(2):
                if rnd == 1:
                    w("tensor", "Sg", cnt["Sg"])
                for j in range(4):
                    kc = rnd * 4 + j
                    tv = E("tensor", lambda e, k=kc, j_=j, m_=m: e.transpose(
                        PD[:, j_ * 128:(j_ + 1) * 128],
                        H[:, m_ * D + k * 128: m_ * D + (k + 1) * 128], IDT), "T")
                w("scalar", "T", tv)
                E("scalar", lambda e, r=rnd: e.copy(
                    HFR[:, r * 512:(r + 1) * 512], PD), "Sg")

        hfr_fill(0)
        for e_i in range(NE):
            for mt in range(NMT):
                wdma("tensor", ld_done[("w1p1", c, e_i)])
                wdma("tensor", ld_done[("w1p2", c, e_i)])
                w("tensor", "Sg", cnt["Sg"])
                t_bank = []
                for nq in range(4):
                    ps = PA if nq < 2 else PB
                    off = (nq % 2) * 512
                    for kc in range(8):
                        wap = w1_ap(c, kc, nq)
                        tv = E("tensor", lambda e, k=kc, p=ps, o=off, a=wap:
                               e.matmul(p[:, o:o + 512],
                                        HFR[:, k * 128:(k + 1) * 128], a,
                                        start=(k == 0), stop=(k == 7)),
                               "T" if kc == 7 else None)
                    t_bank.append(tv)
                    w("vector", "T", tv)
                    E("vector", lambda e, n_=nq, p=ps, o=off: e.bn_stats(
                        stats_raw[:, n_ * 6:(n_ + 1) * 6], p[:, o:o + 512]))
                t_hid = t_bank[3]
                if not (e_i == NE - 1 and mt == NMT - 1):
                    hfr_fill((mt + 1) % NMT)
                if mt == NMT - 1:
                    rel[("w1_free", c, e_i)] = t_hid
                    if e_i < NE - 1:
                        w("sync", "T", t_hid)
                        load_w1p1(c, e_i + 1)
                        load_w1p2(c, e_i + 1)
                    else:
                        w("sync", "T", t_hid)
                        if c < n_chunks - 1:
                            load_wq(c + 1)
                            load_w1p1(c + 1, 0)

                vbar(lambda e: e.bn_aggr(ST[:, 24:26], stats_raw[:, 0:24]))
                vbar(lambda e: e.tensor_copy(st_mu, ST[:, 24:25]))
                vv = E("vector", lambda e: e.tensor_copy(st_var, ST[:, 25:26]), "V")
                w("vector", "V", vv)
                w("scalar", "V", vv)
                sv = E("scalar", lambda e: e.activation(
                    out=st_sstd, in_=st_var, func=AF.Sqrt, bias=c_eps, scale=1.0),
                    "Sg")
                w("vector", "Sg", sv)
                vbar(lambda e: e.reciprocal(st_rstd, st_sstd))
                v_nm = vbar(lambda e: e.tensor_scalar(
                    st_negmur, st_mu, st_rstd, -1.0, op0=ALU.mult, op1=ALU.mult))
                v_hidn = E("vector", lambda e: e.tensor_scalar(
                    HIDN[:, 0:1024], PA, st_rstd, st_negmur,
                    op0=ALU.mult, op1=ALU.add), "V")
                w("scalar", "V", v_nm)
                s_m = E("scalar", lambda e: e.mul(out=HIDN[:, 1024:2048], in_=PB,
                                                  mul=st_rstd), "Sg")
                w("scalar", "Sg", s_m)
                s_hidn = E("scalar", lambda e: e.add(out=HIDN[:, 1024:2048],
                                                     in_=HIDN[:, 1024:2048],
                                                     add=st_negmur), "Sg")
                w("tensor", "V", v_hidn)
                w("tensor", "Sg", s_hidn)

                wdma("tensor", ld_done[("w2", c, e_i)])
                ev = {}

                def eo_tr(h, ei):
                    s_ = h % 2
                    pdst = PD[:, 256:384] if s_ == 0 else PE[:, 256:384]
                    tv = E("tensor", lambda e, h_=h, p=pdst: e.transpose(
                        p, HIDN[:, h_ * 128:(h_ + 1) * 128], IDT), "T")
                    w("scalar", "T", tv)
                    ev[h] = E("scalar", lambda e, h_=h, ss=s_, p=pdst, ei_=ei:
                              e.activation(
                        out=HDR[:, ss * 128:(ss + 1) * 128], in_=p,
                        func=AF.Relu, bias=LNB[:, ei_ * 16 + h_:ei_ * 16 + h_ + 1],
                        scale=LNG[:, ei_ * 16 + h_:ei_ * 16 + h_ + 1]), "Sg")

                eo_tr(0, e_i)
                for hc in range(16):
                    if hc + 1 < 16:
                        eo_tr(hc + 1, e_i)
                    w("tensor", "Sg", ev[hc])
                    for half in range(2):
                        wap = w2_ap(c, hc, half)
                        tv = E("tensor", lambda e, h=hc, s_=hc % 2, hf=half, a=wap:
                               e.matmul(PC[:, hf * 512:(hf + 1) * 512],
                                        HDR[:, s_ * 128:(s_ + 1) * 128], a,
                                        start=(h == 0), stop=(h == 15)),
                               "T" if (hc == 15 and half == 1) else None)
                t_eo = tv
                if mt == NMT - 1:
                    rel[("w2_free", c, e_i)] = t_eo
                    w("sync", "T", t_eo)
                    if e_i < NE - 1:
                        load_w2(c, e_i + 1)
                    elif c < n_chunks - 1:
                        load_wkv(c + 1)

                w("vector", "T", t_eo)
                if e_i == 0:
                    vv = E("vector", lambda e, m=mt: e.tensor_scalar(
                        ACC[:, m * D:(m + 1) * D], PC,
                        C4[:, m * NE:m * NE + 1], None, op0=ALU.mult), "V")
                else:
                    vv = E("vector", lambda e, m=mt, ei=e_i: e.scalar_tensor_tensor(
                        ACC[:, m * D:(m + 1) * D], PC,
                        C4[:, m * NE + ei:m * NE + ei + 1],
                        ACC[:, m * D:(m + 1) * D],
                        op0=ALU.mult, op1=ALU.add), "V")
                w("tensor", "V", vv)

        if debug is not None and c == debug:
            w("gpsimd", "V", cnt["V"])
            dma_group("st", [(dbgH_d, H), (dbgC_d, C4), (dbgA_d, ACC)])
        w("sync", "T", cnt["T"])
        w("sync", "V", cnt["V"])
        tok_on = dma_group("ld", [(ROWD, ong_d)])
        wdma("tensor", tok_on)
        for hh in range(2):
            tv = E("tensor", lambda e, h=hh: e.matmul(
                PA[:, h * 512:(h + 1) * 512], ONES_R, ROWD[:, h * 512:(h + 1) * 512],
                start=True, stop=True), "T")
        w("sync", "T", tv)
        tok_on = dma_group("ld", [(ROWD, onb_d)])
        wdma("tensor", tok_on)
        for hh in range(2):
            tv = E("tensor", lambda e, h=hh: e.matmul(
                PB[:, h * 512:(h + 1) * 512], ONES_R, ROWD[:, h * 512:(h + 1) * 512],
                start=True, stop=True), "T")
        t_onb = tv
        w("vector", "T", t_onb)
        for mt in range(NMT):
            r0 = c * CH + mt * WIN
            layer_stats(ACC[:, mt * D:(mt + 1) * D], D)
            E("vector", lambda e, m=mt: e.tensor_scalar(
                FN, ACC[:, m * D:(m + 1) * D], st_rstd, st_negmur,
                op0=ALU.mult, op1=ALU.add))
            E("vector", lambda e: e.tensor_mul(FN, FN, PA))
            E("vector", lambda e: e.tensor_add(FN, FN, PB))
            if c > 0 or mt > 0:
                wdma_all("vector", "st")
            E("vector", lambda e, m=mt: e.tensor_scalar(
                OUT, H[:, m * D:(m + 1) * D],
                MUSIG[:, 2 * m + 1:2 * m + 2], MUSIG[:, 2 * m:2 * m + 1],
                op0=ALU.mult, op1=ALU.add))
            vv = E("vector", lambda e: e.tensor_add(OUT, OUT, FN), "V")
            w("gpsimd", "V", vv)
            dma_group("st", [(y_d[r0:r0 + 128, :], OUT)])
        rel[("onfree", c)] = vv

    wdma_all("gpsimd", "st")
    wdma_all("gpsimd", "x0")
    wdma_all("sync", "ld")

    with nc.Block() as block:
        @block.sync
        def _(e):
            for fn in q["sync"]:
                fn(e)

        @block.tensor
        def _(e):
            for fn in q["tensor"]:
                fn(e)

        @block.vector
        def _(e):
            for fn in q["vector"]:
                fn(e)

        @block.scalar
        def _(e):
            for fn in q["scalar"]:
                fn(e)

        @block.gpsimd
        def _(e):
            for fn in q["gpsimd"]:
                fn(e)

    stack.close()
    return nc


def host_inputs(inp):
    f32 = np.float32
    asc = np.ascontiguousarray

    def t(a):
        return asc(np.asarray(a, dtype=f32).T)

    def pp(a):
        return asc(np.asarray(a, dtype=f32).reshape(8, 128).T)

    def pp16(a):
        return asc(np.asarray(a, dtype=f32).reshape(4, 16, 128).transpose(2, 0, 1)
                   .reshape(128, 64))

    def row(a):
        return asc(np.asarray(a, dtype=f32).reshape(1, -1))

    shared = {
        "wqT": t(inp["L0_attn_Wq"]),
        "wkT": t(inp["L0_attn_Wk"]),
        "wvT": t(inp["L0_attn_Wv"]),
        "bq": pp(inp["L0_attn_bq"]),
        "bk": pp(inp["L0_attn_bk"]),
        "bv": row(inp["L0_attn_bv"]),
        "gam": row(inp["L0_attn_gamma"]),
        "bet": row(inp["L0_attn_beta"]),
        "rwT": t(inp["L0_ffn_router_w"]),
        "rng": pp(inp["L0_ffn_rn_g"]),
        "rnb": pp(inp["L0_ffn_rn_b"]),
        "w1": asc(np.asarray(inp["L0_ffn_w1"], dtype=f32)),
        "lng": pp16(inp["L0_ffn_ln_g"]),
        "lnb": pp16(inp["L0_ffn_ln_b"]),
        "w2": asc(np.asarray(inp["L0_ffn_w2"], dtype=f32)),
        "ong": row(inp["L0_ffn_on_g"]),
        "onb": row(inp["L0_ffn_on_b"]),
        "ident": np.eye(128, dtype=f32),
    }
    return shared


_NC_CACHE = {}


def kernel(**inputs):
    full_x = np.asarray(inputs["L0_attn_x"], dtype=np.float32)
    shared = host_inputs(inputs)
    if "nc" not in _NC_CACHE:
        _NC_CACHE["nc"] = build_nc(n_chunks=8)
    nc = _NC_CACHE["nc"]
    from concourse.bass_utils import run_bass_kernel_spmd
    in_maps = []
    for b in range(8):
        m = dict(shared)
        m["x"] = np.ascontiguousarray(full_x[b])
        in_maps.append(m)
    res = run_bass_kernel_spmd(nc, in_maps, core_ids=list(range(8)))
    out = np.stack([np.asarray(res.results[b]["y"], dtype=np.float32)
                    for b in range(8)], axis=0)
    return out


# version 08
# speedup vs baseline: 1.1071x; 1.1071x over previous
import numpy as np
import concourse.bass as bass
import concourse.mybir as mybir

F32 = mybir.dt.float32
BF16 = mybir.dt.bfloat16
AF = mybir.ActivationFunctionType
ALU = mybir.AluOpType
AX = mybir.AxisListType
NEG = -1e30

D = 1024
HID = 2048
NE = 4
WIN = 128
CH = 512
NMT = CH // WIN
S = 4096
EPS = 1e-5
EPSG = 1e-8
NSLOPE = 0.01
INV_SQRT_H = 1.0 / 32.0

REG = 8192
RA, RB, RC, RD = 0, REG, 2 * REG, 3 * REG


def build_nc(n_chunks=8, debug=None):
    nc = bass.Bass("TRN2", target_bir_lowering=False)

    x_d = nc.dram_tensor("x", [S, D], F32, kind="ExternalInput").ap()
    wqT_d = nc.dram_tensor("wqT", [D, D], F32, kind="ExternalInput").ap()
    wkT_d = nc.dram_tensor("wkT", [D, D], F32, kind="ExternalInput").ap()
    wvT_d = nc.dram_tensor("wvT", [D, D], F32, kind="ExternalInput").ap()
    bq_d = nc.dram_tensor("bq", [128, 8], F32, kind="ExternalInput").ap()
    bk_d = nc.dram_tensor("bk", [128, 8], F32, kind="ExternalInput").ap()
    bv_d = nc.dram_tensor("bv", [1, D], F32, kind="ExternalInput").ap()
    gam_d = nc.dram_tensor("gam", [1, D], F32, kind="ExternalInput").ap()
    bet_d = nc.dram_tensor("bet", [1, D], F32, kind="ExternalInput").ap()
    rwT_d = nc.dram_tensor("rwT", [D, NE], F32, kind="ExternalInput").ap()
    rng_d = nc.dram_tensor("rng", [128, 8], F32, kind="ExternalInput").ap()
    rnb_d = nc.dram_tensor("rnb", [128, 8], F32, kind="ExternalInput").ap()
    w1_d = nc.dram_tensor("w1", [NE, D, HID], F32, kind="ExternalInput").ap()
    lng_d = nc.dram_tensor("lng", [128, NE * 16], F32, kind="ExternalInput").ap()
    lnb_d = nc.dram_tensor("lnb", [128, NE * 16], F32, kind="ExternalInput").ap()
    w2_d = nc.dram_tensor("w2", [NE, HID, D], F32, kind="ExternalInput").ap()
    ong_d = nc.dram_tensor("ong", [1, D], F32, kind="ExternalInput").ap()
    onb_d = nc.dram_tensor("onb", [1, D], F32, kind="ExternalInput").ap()
    ident_d = nc.dram_tensor("ident", [128, 128], F32, kind="ExternalInput").ap()
    w1hi_d = nc.dram_tensor("w1hi", [NE, D, HID], BF16, kind="Internal").ap()
    w1lo_d = nc.dram_tensor("w1lo", [NE, D, HID], BF16, kind="Internal").ap()
    y_d = nc.dram_tensor("y", [S, D], F32, kind="ExternalOutput").ap()
    if debug:
        dbgH_d = nc.dram_tensor("dbgH", [128, NMT * D], F32, kind="ExternalOutput").ap()
        dbgC_d = nc.dram_tensor("dbgC", [128, NE * NMT], F32, kind="ExternalOutput").ap()
        dbgA_d = nc.dram_tensor("dbgA", [128, NMT * D], F32, kind="ExternalOutput").ap()

    SLOT = nc.alloc_sbuf_tensor("SLOT", [128, 4 * REG], F32).ap()
    H = nc.alloc_sbuf_tensor("H", [128, NMT * D], F32).ap()
    ACC = nc.alloc_sbuf_tensor("ACC", [128, NMT * D], F32).ap()
    X = nc.alloc_sbuf_tensor("X", [128, 2 * D], F32).ap()
    QF = nc.alloc_sbuf_tensor("QF", [128, D], F32).ap()
    KF = nc.alloc_sbuf_tensor("KF", [128, D], F32).ap()
    VT = nc.alloc_sbuf_tensor("VT", [128, D], F32).ap()
    M8 = nc.alloc_sbuf_tensor("M8", [128, 8], F32).ap()
    M8B = nc.alloc_sbuf_tensor("M8B", [128, 8], F32).ap()
    HIDN = nc.alloc_sbuf_tensor("HIDN", [128, HID], F32).ap()
    HFR = nc.alloc_sbuf_tensor("HFR", [128, 8 * 128], F32).ap()
    HDR = nc.alloc_sbuf_tensor("HDR", [128, 2 * 128], F32).ap()
    SC = HFR[:, 256:384]
    EA = HFR[:, 384:512]
    PP = HFR[:, 512:640]
    PT = HFR[:, 640:768]
    ZA = HDR[:, 0:128]
    ZB = HDR[:, 128:256]
    BPAR = nc.alloc_sbuf_tensor("BPAR", [128, 2 * D], F32).ap()
    BVB = nc.alloc_sbuf_tensor("BVB", [128, D], F32).ap()
    L1F = HIDN[:, 0:D]
    L1 = HIDN[:, D:2 * D]
    U = VT
    FN = QF
    OUT = KF

    BQ = nc.alloc_sbuf_tensor("BQ", [128, 8], F32).ap()
    BK = nc.alloc_sbuf_tensor("BK", [128, 8], F32).ap()
    RNG = nc.alloc_sbuf_tensor("RNG", [128, 8], F32).ap()
    RNB = nc.alloc_sbuf_tensor("RNB", [128, 8], F32).ap()
    LNG = nc.alloc_sbuf_tensor("LNG", [128, NE * 16], F32).ap()
    LNB = nc.alloc_sbuf_tensor("LNB", [128, NE * 16], F32).ap()
    RWT = nc.alloc_sbuf_tensor("RWT", [128, 8 * NE], F32).ap()
    GT = nc.alloc_sbuf_tensor("GT", [128, 8 * NE], F32).ap()
    TB = nc.alloc_sbuf_tensor("TB", [128, 8 * NE], F32).ap()
    GB4 = nc.alloc_sbuf_tensor("GB4", [128, NE], F32).ap()
    CB4 = nc.alloc_sbuf_tensor("CB4", [128, NE], F32).ap()
    GC1 = nc.alloc_sbuf_tensor("GC1", [1, 2 * NE], F32).ap()
    ROWD = QF[0:1, 0:D]
    ONES_R = nc.alloc_sbuf_tensor("ONES_R", [1, 128], F32).ap()
    ONES_C = nc.alloc_sbuf_tensor("ONES_C", [128, 1], F32).ap()
    IDT = nc.alloc_sbuf_tensor("IDT", [128, 128], F32).ap()
    CONST = nc.alloc_sbuf_tensor("CONST", [128, 4], F32).ap()
    MUSIG = nc.alloc_sbuf_tensor("MUSIG", [128, 2 * NMT], F32).ap()
    C4 = nc.alloc_sbuf_tensor("C4", [128, NE * NMT], F32).ap()
    ST = nc.alloc_sbuf_tensor("ST", [128, 64], F32).ap()
    S4 = nc.alloc_sbuf_tensor("S4", [128, 8 * NE], F32).ap()


    def _addr(name):
        import concourse.mybir as _mb
        for al in nc.m.functions[0].allocations:
            if isinstance(al, _mb.MemoryLocationSet) and al.memorylocations and \
                    al.memorylocations[0].name == name:
                return al.memorylocations[0].addr
        raise KeyError(name)

    SLOT16 = nc.alloc_sbuf_tensor_at("SLOT16", [128, 8 * REG], BF16,
                                     offset=_addr("SLOT")).ap()
    HFR16 = nc.alloc_sbuf_tensor_at("HFR16", [128, 2048], BF16,
                                    offset=_addr("HFR")).ap()
    STG_W = nc.alloc_sbuf_tensor_at("STG_W", [128, 2048], F32,
                                    offset=_addr("ACC")).ap()
    STG_HI = nc.alloc_sbuf_tensor_at("STG_HI", [128, 2048], BF16,
                                     offset=_addr("ACC") + 8192).ap()
    STG_LO = nc.alloc_sbuf_tensor_at("STG_LO", [128, 2048], BF16,
                                     offset=_addr("ACC") + 12288).ap()

    PA = nc.alloc_psum_tensor("PA", [128, 1024], F32).ap()
    PB = nc.alloc_psum_tensor("PB", [128, 1024], F32).ap()
    PC = nc.alloc_psum_tensor("PC", [128, 1024], F32).ap()
    PD = nc.alloc_psum_tensor("PD", [128, 512], F32).ap()
    PE = nc.alloc_psum_tensor("PE", [128, 512], F32).ap()

    q = {"sync": [], "tensor": [], "vector": [], "scalar": [], "gpsimd": []}
    cnt = {"T": 0, "V": 0, "Sg": 0, "G": 0}

    import contextlib
    stack = contextlib.ExitStack()
    sT = stack.enter_context(nc.semaphore("sT"))
    sV = stack.enter_context(nc.semaphore("sV"))
    sS = stack.enter_context(nc.semaphore("sSg"))
    sG = stack.enter_context(nc.semaphore("sGp"))
    SEM = {"T": sT, "V": sV, "Sg": sS, "G": sG}
    LDP = [stack.enter_context(nc.semaphore(f"sL{i}")) for i in range(8)]
    XP = [stack.enter_context(nc.semaphore(f"sX{i}")) for i in range(2)]
    OP = [stack.enter_context(nc.semaphore(f"sO{i}")) for i in range(2)]
    dq = {
        "ld": {"q": "sync", "sems": LDP, "tot": [0] * len(LDP), "g": 0},
        "x0": {"q": "gpsimd", "sems": XP, "tot": [0] * len(XP), "g": 0},
        "st": {"q": "gpsimd", "sems": OP, "tot": [0] * len(OP), "g": 0},
    }

    def emit(eng, fn):
        q[eng].append(fn)

    def inc(name, k=1):
        cnt[name] += k
        return cnt[name]

    def w(eng, name, val):
        assert val <= cnt[name], (eng, name, val, cnt[name])
        emit(eng, lambda e, s=SEM[name], v=val: e.wait_ge(s, v))

    def E(eng, fn, inc_sem=None, inck=1):
        if eng == "vector" and inc_sem is None:
            inc_sem = "V"
        if inc_sem is None:
            emit(eng, fn)
            return None
        v = inc(inc_sem, inck)
        emit(eng, lambda e, f=fn, s=SEM[inc_sem], k=inck: f(e).then_inc(s, k))
        if eng == "vector":
            w("vector", "V", cnt["V"])
        return v

    def vbar(fn):
        v = E("vector", fn, "V")
        w("vector", "V", v)
        return v

    def dma_group(pool, pairs):
        d = dq[pool]
        i = d["g"] % len(d["sems"])
        d["g"] += 1
        s = d["sems"][i]
        prev = d["tot"][i]
        if prev:
            emit(d["q"], lambda e, s_=s, v=prev: e.wait_ge(s_, v))
        for o, inp in pairs:
            d["tot"][i] += 16
            emit(d["q"], lambda e, o_=o, i_=inp, s_=s:
                 e.dma_start(out=o_, in_=i_).then_inc(s_, 16))
        return (pool, i, d["tot"][i])

    def wdma(eng, tok):
        pool, i, v = tok
        emit(eng, lambda e, s=dq[pool]["sems"][i], v_=v: e.wait_ge(s, v_))

    def wdma_all(eng, pool):
        d = dq[pool]
        for i, s in enumerate(d["sems"]):
            if d["tot"][i]:
                emit(eng, lambda e, s_=s, v=d["tot"][i]: e.wait_ge(s_, v))

    c_eps = CONST[:, 0:1]
    c_i32 = CONST[:, 1:2]

    stats_raw = ST[:, 0:24]
    st_mu = ST[:, 26:27]
    st_var = ST[:, 27:28]
    st_sstd = ST[:, 28:29]
    st_rstd = ST[:, 29:30]
    st_negmur = ST[:, 30:31]

    def layer_stats(src_ap, width, psum=False, mu_out=None, rstd_out=None,
                    sstd_out=None):
        mu_o = mu_out if mu_out is not None else st_mu
        sstd_o = sstd_out if sstd_out is not None else st_sstd
        rstd_o = rstd_out if rstd_out is not None else st_rstd
        nch = (width + 511) // 512
        for i in range(nch):
            c0 = i * 512
            c1 = min(width, c0 + 512)
            vv = E("vector", lambda e, s=src_ap[:, c0:c1], o=stats_raw[:, i * 6:(i + 1) * 6]:
                   e.bn_stats(o, s), "V")
        w("vector", "V", cnt["V"])
        vbar(lambda e, i=nch: e.bn_aggr(ST[:, 24:26], stats_raw[:, 0:6 * i]))
        vbar(lambda e, o=mu_o: e.tensor_copy(o, ST[:, 24:25]))
        vv = E("vector", lambda e: e.tensor_copy(st_var, ST[:, 25:26]), "V")
        w("vector", "V", vv)
        w("scalar", "V", vv)
        sv = E("scalar", lambda e, o=sstd_o: e.activation(
            out=o, in_=st_var, func=AF.Sqrt, bias=c_eps, scale=1.0), "Sg")
        w("vector", "Sg", sv)
        vbar(lambda e, o=rstd_o, i=sstd_o: e.reciprocal(o, i))
        vbar(lambda e, m=mu_o, r=rstd_o: e.tensor_scalar(
            st_negmur, m, r, -1.0, op0=ALU.mult, op1=ALU.mult))

    const_pairs = [(BQ, bq_d), (BK, bk_d), (RNG, rng_d), (RNB, rnb_d),
                   (LNG, lng_d), (LNB, lnb_d), (IDT, ident_d), (ROWD, bv_d)]
    for kc in range(8):
        const_pairs.append((RWT[:, kc * NE:(kc + 1) * NE],
                            rwT_d[kc * 128:(kc + 1) * 128, :]))
    tok_const = dma_group("ld", const_pairs)

    E("vector", lambda e: e.memset(CONST[:, 0:1], EPS))
    E("vector", lambda e: e.memset(CONST[:, 1:2], INV_SQRT_H))
    E("vector", lambda e: e.memset(ONES_R, 1.0))
    E("vector", lambda e: e.memset(ONES_C, 1.0))
    wdma("vector", tok_const)
    for kc in range(8):
        E("vector", lambda e, k=kc: e.tensor_scalar(
            GT[:, k * NE:(k + 1) * NE], RWT[:, k * NE:(k + 1) * NE],
            RNG[:, k:k + 1], None, op0=ALU.mult))
        E("vector", lambda e, k=kc: e.tensor_scalar(
            TB[:, k * NE:(k + 1) * NE], RWT[:, k * NE:(k + 1) * NE],
            RNB[:, k:k + 1], None, op0=ALU.mult))
    vlast = E("vector", lambda e: e.engine_nop(), "V")
    w("tensor", "V", vlast)
    for kc in range(8):
        tv = E("tensor", lambda e, k=kc: e.matmul(
            PE[0:1, 0:NE], ONES_C, GT[:, k * NE:(k + 1) * NE],
            start=(k == 0), stop=(k == 7)), "T" if kc == 7 else None)
    w("vector", "T", tv)
    E("vector", lambda e: e.tensor_copy(GC1[:, 0:NE], PE[0:1, 0:NE]), "V")
    vv = E("vector", lambda e: e.engine_nop(), "V")
    w("tensor", "V", vv)
    for kc in range(8):
        tv = E("tensor", lambda e, k=kc: e.matmul(
            PE[0:1, NE:2 * NE], ONES_C, TB[:, k * NE:(k + 1) * NE],
            start=(k == 0), stop=(k == 7)), "T" if kc == 7 else None)
    w("vector", "T", tv)
    vv = E("vector", lambda e: e.tensor_copy(GC1[:, NE:2 * NE], PE[0:1, NE:2 * NE]), "V")
    w("tensor", "V", vv)
    tv = E("tensor", lambda e: e.matmul(PE[:, 0:2 * NE], ONES_R, GC1,
                                        start=True, stop=True), "T")
    w("vector", "T", tv)
    E("vector", lambda e: e.tensor_copy(GB4, PE[:, 0:NE]))
    vv = E("vector", lambda e: e.tensor_copy(CB4, PE[:, NE:2 * NE]), "V")
    w("tensor", "V", vv)

    def bcast_row(dst_ap, row_dram, tag):
        w("sync", "T", cnt["T"])
        w("sync", "V", cnt["V"])
        tok = dma_group("ld", [(ROWD, row_dram)])
        wdma("tensor", tok)
        for hh in range(2):
            tv = E("tensor", lambda e, h=hh: e.matmul(
                PE, ONES_R, ROWD[:, h * 512:(h + 1) * 512],
                start=True, stop=True), "T")
            w("vector", "T", tv)
            vv = E("vector", lambda e, h=hh, d=dst_ap: e.tensor_copy(
                d[:, h * 512:(h + 1) * 512], PE), "V")
            w("tensor", "V", vv)
        return vv

    for hh in range(2):
        tv = E("tensor", lambda e, h=hh: e.matmul(
            PE, ONES_R, ROWD[:, h * 512:(h + 1) * 512], start=True, stop=True), "T")
        w("vector", "T", tv)
        vv = E("vector", lambda e, h=hh: e.tensor_copy(
            BVB[:, h * 512:(h + 1) * 512], PE), "V")
        w("tensor", "V", vv)
    bcast_row(BPAR[:, 0:D], gam_d, "gam")
    last_bpar_v = bcast_row(BPAR[:, D:2 * D], bet_d, "bet")

    def wq_base(c):
        return RA if c % 2 == 0 else RD

    def w1p1_base(c):
        return RD if c % 2 == 0 else RA

    ld_done = {}

    def load_wq(c):
        base = wq_base(c)
        ld_done[("wq", c)] = dma_group("ld", [
            (SLOT[:, base + kc * D: base + (kc + 1) * D],
             wqT_d[kc * 128:(kc + 1) * 128, :]) for kc in range(8)])

    def load_wkv(c):
        ld_done[("wk", c)] = dma_group("ld", [
            (SLOT[:, RB + kc * D: RB + (kc + 1) * D],
             wkT_d[kc * 128:(kc + 1) * 128, :]) for kc in range(8)])
        ld_done[("wv", c)] = dma_group("ld", [
            (SLOT[:, RC + kc * D: RC + (kc + 1) * D],
             wvT_d[kc * 128:(kc + 1) * 128, :]) for kc in range(8)])

    def load_w1p1(c, e):
        base = 2 * w1p1_base(c)
        if ("split", e) in rel:
            wdma("sync", rel[("split", e)])
        ld_done[("w1p1", c, e)] = dma_group("ld", [
            (SLOT16[:, base + kc * HID: base + (kc + 1) * HID],
             w1hi_d[e, kc * 128:(kc + 1) * 128, :]) for kc in range(8)])

    def load_w1p2(c, e):
        base = 2 * wq_base(c)
        if ("split", e) in rel:
            wdma("sync", rel[("split", e)])
        ld_done[("w1p2", c, e)] = dma_group("ld", [
            (SLOT16[:, base + kc * HID: base + (kc + 1) * HID],
             w1lo_d[e, kc * 128:(kc + 1) * 128, :]) for kc in range(8)])

    def load_w2(c, e):
        ld_done[("w2", c, e)] = dma_group("ld", [
            (SLOT[:, RB + hc * D: RB + (hc + 1) * D],
             w2_d[e, hc * 128:(hc + 1) * 128, :]) for hc in range(16)])

    def wq_ap(c, kc, oc):
        base = wq_base(c)
        return SLOT[:, base + kc * D + oc * 128: base + kc * D + (oc + 1) * 128]

    def wk_ap(c, kc, oc):
        return SLOT[:, RB + kc * D + oc * 128: RB + kc * D + (oc + 1) * 128]

    def wv_ap(c, kc, half):
        return SLOT[:, RC + kc * D + half * 512: RC + kc * D + (half + 1) * 512]

    def w1hi_ap(c, kc, nq):
        base = 2 * w1p1_base(c)
        return SLOT16[:, base + kc * HID + nq * 512: base + kc * HID + (nq + 1) * 512]

    def w1lo_ap(c, kc, nq):
        base = 2 * wq_base(c)
        return SLOT16[:, base + kc * HID + nq * 512: base + kc * HID + (nq + 1) * 512]

    def w2_ap(c, hc, half):
        return SLOT[:, RB + hc * D + half * 512: RB + hc * D + (half + 1) * 512]

    rel = {}

    strip_list = [(e, kc) for e in range(NE) for kc in range(8)]
    strip_state = {"i": 0, "prev_out": None}

    def emit_strips(n):
        for _ in range(n):
            if strip_state["i"] >= len(strip_list):
                return
            e, kc = strip_list[strip_state["i"]]
            strip_state["i"] += 1
            if strip_state["prev_out"] is not None:
                wdma("sync", strip_state["prev_out"])
            tin = dma_group("ld", [(STG_W, w1_d[e, kc * 128:(kc + 1) * 128, :])])
            wdma("gpsimd", tin)
            g1 = E("gpsimd", lambda ee: ee.tensor_copy(STG_HI, STG_W), "G")
            w("gpsimd", "G", g1)
            g2 = E("gpsimd", lambda ee: ee.tensor_tensor(
                STG_LO, STG_W, STG_HI, op=ALU.subtract), "G")
            w("gpsimd", "G", g2)
            tout = dma_group("st", [
                (w1hi_d[e, kc * 128:(kc + 1) * 128, :], STG_HI),
                (w1lo_d[e, kc * 128:(kc + 1) * 128, :], STG_LO)])
            strip_state["prev_out"] = tout
            if kc == 7:
                rel[("split", e)] = tout

    load_wq(0)
    load_wkv(0)

    for c in range(n_chunks):
        wdma_all("scalar", "st")
        arel = {}

        def st_ln1(i):
            xb = X[:, (i % 2) * D:(i % 2 + 1) * D]
            key = ("xfree", c, i - 2) if i >= 2 else ("xfree", c - 1, i + 2)
            if key in rel:
                w("gpsimd", "V", rel[key])
            r0 = c * CH + i * WIN
            tok_x0 = dma_group("x0", [(xb, x_d[r0:r0 + 128, :])])
            wdma("vector", tok_x0)
            layer_stats(xb, D)
            arel[("L1", i)] = E("vector", lambda e, x=xb: e.tensor_scalar(
                L1, x, st_rstd, st_negmur, op0=ALU.mult, op1=ALU.add), "V")

        def st_ln1tr(i):
            w("tensor", "V", arel[("L1", i)])
            for rnd in range(2):
                if rnd == 1:
                    w("tensor", "Sg", cnt["Sg"])
                for j in range(4):
                    kc = rnd * 4 + j
                    tv = E("tensor", lambda e, k=kc, j_=j: e.transpose(
                        PD[:, j_ * 128:(j_ + 1) * 128],
                        L1[:, k * 128:(k + 1) * 128], IDT), "T")
                w("scalar", "T", tv)
                E("scalar", lambda e, r=rnd: e.copy(
                    L1F[:, r * 512:(r + 1) * 512], PD), "Sg")
            w("tensor", "Sg", cnt["Sg"])

        def st_qk(i):
            if i == 0 and ("onfree", c - 1) in rel:
                w("tensor", "V", rel[("onfree", c - 1)])
            wdma("tensor", ld_done[("wq", c)])
            t_q0 = t_q1 = None
            for oc in range(8):
                for kc in range(8):
                    wap = wq_ap(c, kc, oc)
                    tv = E("tensor", lambda e, o=oc, k=kc, a=wap: e.matmul(
                        PA[:, o * 128:(o + 1) * 128], a,
                        L1F[:, k * 128:(k + 1) * 128],
                        start=(k == 0), stop=(k == 7)),
                        "T" if (oc in (3, 7) and kc == 7) else None)
                    if oc == 3 and kc == 7:
                        t_q0 = tv
                    if oc == 7 and kc == 7:
                        t_q1 = tv
            if i == NMT - 1:
                rel[("wq_free", c)] = t_q1
            w("scalar", "T", t_q0)
            for oc in range(4):
                E("scalar", lambda e, o=oc: e.add(
                    out=QF[:, o * 128:(o + 1) * 128],
                    in_=PA[:, o * 128:(o + 1) * 128], add=BQ[:, o:o + 1]), "Sg")
            w("scalar", "T", t_q1)
            for oc in range(4, 8):
                E("scalar", lambda e, o=oc: e.add(
                    out=QF[:, o * 128:(o + 1) * 128],
                    in_=PA[:, o * 128:(o + 1) * 128], add=BQ[:, o:o + 1]), "Sg")
            wdma("tensor", ld_done[("wk", c)])
            t_k0 = t_k1 = None
            for oc in range(8):
                for kc in range(8):
                    wap = wk_ap(c, kc, oc)
                    tv = E("tensor", lambda e, o=oc, k=kc, a=wap: e.matmul(
                        PB[:, o * 128:(o + 1) * 128], a,
                        L1F[:, k * 128:(k + 1) * 128],
                        start=(k == 0), stop=(k == 7)),
                        "T" if (oc in (3, 7) and kc == 7) else None)
                    if oc == 3 and kc == 7:
                        t_k0 = tv
                    if oc == 7 and kc == 7:
                        t_k1 = tv
            if i == NMT - 1:
                rel[("wk_free", c)] = t_k1
            w("scalar", "T", t_k0)
            for oc in range(4):
                E("scalar", lambda e, o=oc: e.add(
                    out=KF[:, o * 128:(o + 1) * 128],
                    in_=PB[:, o * 128:(o + 1) * 128], add=BK[:, o:o + 1]), "Sg")
            w("scalar", "T", t_k1)
            sv = None
            for oc in range(4, 8):
                sv = E("scalar", lambda e, o=oc: e.add(
                    out=KF[:, o * 128:(o + 1) * 128],
                    in_=PB[:, o * 128:(o + 1) * 128], add=BK[:, o:o + 1]), "Sg")
            arel[("kev", i)] = sv

        def st_scores(i):
            w("tensor", "Sg", arel[("kev", i)])
            key = ("scfree", c, i - 1) if i >= 1 else ("scfree", c - 1, NMT - 1)
            if key in rel:
                w("tensor", "V", rel[key])
            for kc in range(8):
                tv = E("tensor", lambda e, k=kc: e.matmul(
                    PE[:, 0:128], QF[:, k * 128:(k + 1) * 128],
                    KF[:, k * 128:(k + 1) * 128],
                    start=(k == 0), stop=(k == 7)), "T" if kc == 7 else None)
            arel[("tsc", i)] = tv

        def st_v(i):
            wdma("tensor", ld_done[("wv", c)])
            tv = None
            for half in range(2):
                for kc in range(8):
                    wap = wv_ap(c, kc, half)
                    tv = E("tensor", lambda e, h=half, k=kc, a=wap: e.matmul(
                        PC[:, h * 512:(h + 1) * 512],
                        L1F[:, k * 128:(k + 1) * 128], a,
                        start=(k == 0), stop=(k == 7)),
                        "T" if kc == 7 else None)
            arel[("tv", i)] = tv
            if i == NMT - 1:
                rel[("wv_free", c)] = tv

        def st_softmax(i):
            w("vector", "T", arel[("tsc", i)])
            rel[("scfree", c, i)] = E(
                "vector", lambda e: e.tensor_copy(SC, PE[:, 0:128]), "V")
            vbar(lambda e: e.max(out=M8, in_=SC))
            vnm = vbar(lambda e: e.tensor_scalar(
                ST[:, 31:32], M8[:, 0:1], -INV_SQRT_H, None, op0=ALU.mult))
            vbar(lambda e: e.match_replace(out=ZA, in_to_replace=M8, in_values=SC,
                                           imm_value=NEG))
            vbar(lambda e: e.max(out=M8, in_=ZA))
            vbar(lambda e: e.match_replace(out=ZB, in_to_replace=M8, in_values=ZA,
                                           imm_value=NEG))
            vbar(lambda e: e.max(out=M8, in_=ZB))
            vbar(lambda e: e.match_replace(out=ZA, in_to_replace=M8, in_values=ZB,
                                           imm_value=NEG))
            vbar(lambda e: e.max(out=M8, in_=ZA))
            vbar(lambda e: e.match_replace(out=ZB, in_to_replace=M8, in_values=ZA,
                                           imm_value=NEG))
            E("vector", lambda e: e.tensor_scalar(
                ZA, ZB, NEG, None, op0=ALU.is_equal), "V")
            w("scalar", "V", vnm)
            sv = E("scalar", lambda e: e.activation(
                out=EA, in_=SC, func=AF.Exp, bias=ST[:, 31:32], scale=c_i32), "Sg")
            w("vector", "Sg", sv)
            key = ("x1", c, i - 1) if i >= 1 else ("x1", c - 1, NMT - 1)
            if key in rel:
                w("scalar", "V", rel[key])
            w("scalar", "T", arel[("tv", i)])
            E("scalar", lambda e: e.copy(VT, PC), "Sg")
            E("vector", lambda e: e.tensor_mul(EA, EA, ZA))
            vbar(lambda e: e.reduce_sum(ST[:, 31:32], EA, axis=AX.X))
            vbar(lambda e: e.reciprocal(ST[:, 30:31], ST[:, 31:32]))
            arel[("pp", i)] = E("vector", lambda e: e.tensor_scalar(
                PP, EA, ST[:, 30:31], None, op0=ALU.mult), "V")

        def st_ptav(i):
            w("tensor", "V", arel[("pp", i)])
            tv = E("tensor", lambda e: e.transpose(PD[:, 0:128], PP, IDT), "T")
            w("scalar", "T", tv)
            sv = E("scalar", lambda e: e.copy(PT, PD[:, 0:128]), "Sg")
            w("tensor", "Sg", sv)
            for half in range(2):
                tv = E("tensor", lambda e, h=half: e.matmul(
                    PC[:, h * 512:(h + 1) * 512], PT,
                    VT[:, h * 512:(h + 1) * 512], start=True, stop=True),
                    "T" if half == 1 else None)
            arel[("tav", i)] = tv

        def st_epiA(i):
            w("vector", "T", arel[("tav", i)])
            E("vector", lambda e: e.tensor_add(U, PC, BVB))
            arel[("leak", i)] = E("vector", lambda e: e.scalar_tensor_tensor(
                U, U, NSLOPE, U, op0=ALU.mult, op1=ALU.max), "V")
            E("vector", lambda e: e.tensor_add(U, U, L1))

        def st_epiB(i):
            xb = X[:, (i % 2) * D:(i % 2 + 1) * D]
            layer_stats(U, D)
            E("vector", lambda e: e.tensor_scalar(
                U, U, st_rstd, st_negmur, op0=ALU.mult, op1=ALU.add))
            E("vector", lambda e: e.tensor_mul(U, U, BPAR[:, 0:D]))
            E("vector", lambda e: e.tensor_add(U, U, BPAR[:, D:2 * D]))
            rel[("x1", c, i)] = E("vector", lambda e, x=xb: e.tensor_add(x, x, U),
                                  "V")
            layer_stats(xb, D, mu_out=MUSIG[:, 2 * i:2 * i + 1],
                        sstd_out=MUSIG[:, 2 * i + 1:2 * i + 2])
            vv = E("vector", lambda e, x=xb, m=i: e.tensor_scalar(
                H[:, m * D:(m + 1) * D], x, st_rstd, st_negmur,
                op0=ALU.mult, op1=ALU.add), "V")
            rel[("xfree", c, i)] = vv
            arel[("H", i)] = vv

        def st_router_T(i):
            w("tensor", "V", arel[("H", i)])
            w("tensor", "V", arel[("leak", i)])
            key = ("s4free", c, i - 1) if i >= 1 else ("s4free", c - 1, NMT - 1)
            if key in rel:
                w("tensor", "V", rel[key])
            ev_r = {}

            def r_tr(k):
                s_ = k % 2
                pdst = PD[:, 0:128] if s_ == 0 else PC[:, 0:128]
                tv = E("tensor", lambda e, k_=k, p=pdst: e.transpose(
                    p, H[:, i * D + k_ * 128: i * D + (k_ + 1) * 128], IDT), "T")
                w("scalar", "T", tv)
                ev_r[k] = E("scalar", lambda e, ss=s_, p=pdst: e.copy(
                    HFR[:, ss * 128:(ss + 1) * 128], p), "Sg")

            r_tr(0)
            for kc in range(8):
                if kc + 1 < 8:
                    r_tr(kc + 1)
                w("tensor", "Sg", ev_r[kc])
                tv = E("tensor", lambda e, k=kc, s_=kc % 2: e.matmul(
                    PE[:, 128:128 + NE], HFR[:, s_ * 128:(s_ + 1) * 128],
                    GT[:, k * NE:(k + 1) * NE],
                    start=(k == 0), stop=(k == 7)), "T" if kc == 7 else None)
            arel[("tsr", i)] = tv

        def st_router_V(i):
            layer_stats(H[:, i * D:(i + 1) * D], D)
            vbar(lambda e: e.tensor_scalar(
                ST[:, 33:34], st_mu, -1.0, None, op0=ALU.mult))
            vbar(lambda e: e.tensor_copy(ST[:, 34:35], st_rstd))
            w("vector", "T", arel[("tsr", i)])
            rel[("s4free", c, i)] = vbar(
                lambda e: e.tensor_copy(S4[:, 0:NE], PE[:, 128:128 + NE]))
            vbar(lambda e: e.scalar_tensor_tensor(
                S4[:, NE:2 * NE], GB4, ST[:, 33:34], S4[:, 0:NE],
                op0=ALU.mult, op1=ALU.add))
            vbar(lambda e: e.scalar_tensor_tensor(
                S4[:, NE:2 * NE], S4[:, NE:2 * NE], ST[:, 34:35], CB4,
                op0=ALU.mult, op1=ALU.add))
            vbar(lambda e: e.reduce_max(ST[:, 35:36], S4[:, NE:2 * NE], axis=AX.X))
            vneg = vbar(lambda e: e.tensor_scalar(
                ST[:, 35:36], ST[:, 35:36], -1.0, None, op0=ALU.mult))
            w("scalar", "V", vneg)
            sv = E("scalar", lambda e: e.activation(
                out=S4[:, 2 * NE:3 * NE], in_=S4[:, NE:2 * NE], func=AF.Exp,
                bias=ST[:, 35:36], scale=1.0, accum_out=ST[:, 36:37]), "Sg")
            w("vector", "Sg", sv)
            vbar(lambda e: e.reciprocal(ST[:, 37:38], ST[:, 36:37]))
            vbar(lambda e: e.tensor_scalar(
                S4[:, 2 * NE:3 * NE], S4[:, 2 * NE:3 * NE], ST[:, 37:38], None,
                op0=ALU.mult))
            vbar(lambda e: e.memset(S4[:, 3 * NE:4 * NE], NEG))
            vbar(lambda e: e.max(out=M8B, in_=S4[:, 2 * NE:4 * NE]))
            vbar(lambda e: e.tensor_tensor(
                ST[:, 42:43], M8B[:, 0:1], M8B[:, 1:2], op=ALU.add))
            vbar(lambda e: e.tensor_scalar(
                ST[:, 42:43], ST[:, 42:43], EPSG, None, op0=ALU.add))
            vbar(lambda e: e.reciprocal(ST[:, 43:44], ST[:, 42:43]))
            vbar(lambda e: e.tensor_scalar(
                S4[:, 3 * NE:4 * NE], S4[:, 2 * NE:3 * NE], M8B[:, 1:2], None,
                op0=ALU.is_ge))
            vbar(lambda e: e.tensor_scalar(
                S4[:, 2 * NE:3 * NE], S4[:, 2 * NE:3 * NE], ST[:, 43:44], None,
                op0=ALU.mult))
            E("vector", lambda e, m=i: e.tensor_mul(
                C4[:, m * NE:(m + 1) * NE], S4[:, 2 * NE:3 * NE],
                S4[:, 3 * NE:4 * NE]), "V")

        st_ln1(0)
        if c == 0:
            emit_strips(4)
        for i in range(NMT):
            st_ln1tr(i)
            st_qk(i)
            st_scores(i)
            if i >= 1:
                st_router_T(i - 1)
            st_v(i)
            st_softmax(i)
            if c == 0:
                emit_strips(4)
                if i == 0:
                    load_w1p1(0, 0)
            st_ptav(i)
            st_epiA(i)
            if i + 1 < NMT:
                st_ln1(i + 1)
            if i >= 1:
                st_router_V(i - 1)
            st_epiB(i)
            if c == 0:
                emit_strips(4)
        st_router_T(NMT - 1)
        st_router_V(NMT - 1)
        if c == 0:
            emit_strips(32)

        w("sync", "T", rel[("wq_free", c)])
        load_w1p2(c, 0)
        w("sync", "T", rel[("wv_free", c)])
        load_w2(c, 0)

        def hfr_fill(m):
            for rnd in range(2):
                if rnd == 1:
                    w("tensor", "Sg", cnt["Sg"])
                    w("tensor", "V", cnt["V"])
                for j in range(4):
                    kc = rnd * 4 + j
                    tv = E("tensor", lambda e, k=kc, j_=j, m_=m: e.transpose(
                        PD[:, j_ * 128:(j_ + 1) * 128],
                        H[:, m_ * D + k * 128: m_ * D + (k + 1) * 128], IDT), "T")
                w("scalar", "T", tv)
                sv = E("scalar", lambda e, r=rnd: e.copy(
                    HFR16[:, r * 512:(r + 1) * 512], PD), "Sg")
                w("vector", "Sg", sv)
                E("vector", lambda e, r=rnd: e.tensor_tensor(
                    HFR16[:, 1024 + r * 512:1024 + (r + 1) * 512], PD,
                    HFR16[:, r * 512:(r + 1) * 512], op=ALU.subtract), "V")

        hfr_fill(0)
        if c == 0:
            w("vector", "G", cnt["G"])
            wdma_all("vector", "st")
        for e_i in range(NE):
            for mt in range(NMT):
                wdma("tensor", ld_done[("w1p1", c, e_i)])
                wdma("tensor", ld_done[("w1p2", c, e_i)])
                w("tensor", "Sg", cnt["Sg"])
                w("tensor", "V", cnt["V"])
                t_bank = []
                for nq in range(4):
                    ps = PA if nq < 2 else PB
                    off = (nq % 2) * 512
                    for kc in range(8):
                        hi_a = w1hi_ap(c, kc, nq)
                        lo_a = w1lo_ap(c, kc, nq)
                        hfr_hi = HFR16[:, kc * 128:(kc + 1) * 128]
                        hfr_lo = HFR16[:, 1024 + kc * 128:1024 + (kc + 1) * 128]
                        E("tensor", lambda e, p=ps, o=off, l=hfr_hi, r=hi_a, k=kc:
                          e.matmul(p[:, o:o + 512], l, r,
                                   start=(k == 0), stop=False))
                        E("tensor", lambda e, p=ps, o=off, l=hfr_hi, r=lo_a:
                          e.matmul(p[:, o:o + 512], l, r, start=False, stop=False))
                        tv = E("tensor", lambda e, p=ps, o=off, l=hfr_lo, r=hi_a,
                               k=kc: e.matmul(p[:, o:o + 512], l, r,
                                              start=False, stop=(k == 7)),
                               "T" if kc == 7 else None)
                    t_bank.append(tv)
                    w("vector", "T", tv)
                    E("vector", lambda e, n_=nq, p=ps, o=off: e.bn_stats(
                        stats_raw[:, n_ * 6:(n_ + 1) * 6], p[:, o:o + 512]))
                t_hid = t_bank[3]
                if not (e_i == NE - 1 and mt == NMT - 1):
                    hfr_fill((mt + 1) % NMT)
                if mt == NMT - 1:
                    rel[("w1_free", c, e_i)] = t_hid
                    if e_i < NE - 1:
                        w("sync", "T", t_hid)
                        load_w1p1(c, e_i + 1)
                        load_w1p2(c, e_i + 1)
                    else:
                        w("sync", "T", t_hid)
                        if c < n_chunks - 1:
                            load_wq(c + 1)
                            load_w1p1(c + 1, 0)

                vbar(lambda e: e.bn_aggr(ST[:, 24:26], stats_raw[:, 0:24]))
                vbar(lambda e: e.tensor_copy(st_mu, ST[:, 24:25]))
                vv = E("vector", lambda e: e.tensor_copy(st_var, ST[:, 25:26]), "V")
                w("vector", "V", vv)
                w("scalar", "V", vv)
                sv = E("scalar", lambda e: e.activation(
                    out=st_sstd, in_=st_var, func=AF.Sqrt, bias=c_eps, scale=1.0),
                    "Sg")
                w("vector", "Sg", sv)
                vbar(lambda e: e.reciprocal(st_rstd, st_sstd))
                v_nm = vbar(lambda e: e.tensor_scalar(
                    st_negmur, st_mu, st_rstd, -1.0, op0=ALU.mult, op1=ALU.mult))
                v_hidn = E("vector", lambda e: e.tensor_scalar(
                    HIDN[:, 0:1024], PA, st_rstd, st_negmur,
                    op0=ALU.mult, op1=ALU.add), "V")
                w("scalar", "V", v_nm)
                s_m = E("scalar", lambda e: e.mul(out=HIDN[:, 1024:2048], in_=PB,
                                                  mul=st_rstd), "Sg")
                w("scalar", "Sg", s_m)
                s_hidn = E("scalar", lambda e: e.add(out=HIDN[:, 1024:2048],
                                                     in_=HIDN[:, 1024:2048],
                                                     add=st_negmur), "Sg")
                w("tensor", "V", v_hidn)
                w("tensor", "Sg", s_hidn)

                wdma("tensor", ld_done[("w2", c, e_i)])
                ev = {}

                def eo_tr(h, ei):
                    s_ = h % 2
                    pdst = PD[:, 256:384] if s_ == 0 else PE[:, 256:384]
                    tv = E("tensor", lambda e, h_=h, p=pdst: e.transpose(
                        p, HIDN[:, h_ * 128:(h_ + 1) * 128], IDT), "T")
                    w("scalar", "T", tv)
                    ev[h] = E("scalar", lambda e, h_=h, ss=s_, p=pdst, ei_=ei:
                              e.activation(
                        out=HDR[:, ss * 128:(ss + 1) * 128], in_=p,
                        func=AF.Relu, bias=LNB[:, ei_ * 16 + h_:ei_ * 16 + h_ + 1],
                        scale=LNG[:, ei_ * 16 + h_:ei_ * 16 + h_ + 1]), "Sg")

                eo_tr(0, e_i)
                for hc in range(16):
                    if hc + 1 < 16:
                        eo_tr(hc + 1, e_i)
                    w("tensor", "Sg", ev[hc])
                    for half in range(2):
                        wap = w2_ap(c, hc, half)
                        tv = E("tensor", lambda e, h=hc, s_=hc % 2, hf=half, a=wap:
                               e.matmul(PC[:, hf * 512:(hf + 1) * 512],
                                        HDR[:, s_ * 128:(s_ + 1) * 128], a,
                                        start=(h == 0), stop=(h == 15)),
                               "T" if (hc == 15 and half == 1) else None)
                t_eo = tv
                if mt == NMT - 1:
                    rel[("w2_free", c, e_i)] = t_eo
                    w("sync", "T", t_eo)
                    if e_i < NE - 1:
                        load_w2(c, e_i + 1)
                    elif c < n_chunks - 1:
                        load_wkv(c + 1)

                w("vector", "T", t_eo)
                if e_i == 0:
                    vv = E("vector", lambda e, m=mt: e.tensor_scalar(
                        ACC[:, m * D:(m + 1) * D], PC,
                        C4[:, m * NE:m * NE + 1], None, op0=ALU.mult), "V")
                else:
                    vv = E("vector", lambda e, m=mt, ei=e_i: e.scalar_tensor_tensor(
                        ACC[:, m * D:(m + 1) * D], PC,
                        C4[:, m * NE + ei:m * NE + ei + 1],
                        ACC[:, m * D:(m + 1) * D],
                        op0=ALU.mult, op1=ALU.add), "V")
                w("tensor", "V", vv)

        if debug is not None and c == debug:
            w("gpsimd", "V", cnt["V"])
            dma_group("st", [(dbgH_d, H), (dbgC_d, C4), (dbgA_d, ACC)])
        w("sync", "T", cnt["T"])
        w("sync", "V", cnt["V"])
        tok_on = dma_group("ld", [(ROWD, ong_d)])
        wdma("tensor", tok_on)
        for hh in range(2):
            tv = E("tensor", lambda e, h=hh: e.matmul(
                PA[:, h * 512:(h + 1) * 512], ONES_R, ROWD[:, h * 512:(h + 1) * 512],
                start=True, stop=True), "T")
        w("sync", "T", tv)
        tok_on = dma_group("ld", [(ROWD, onb_d)])
        wdma("tensor", tok_on)
        for hh in range(2):
            tv = E("tensor", lambda e, h=hh: e.matmul(
                PB[:, h * 512:(h + 1) * 512], ONES_R, ROWD[:, h * 512:(h + 1) * 512],
                start=True, stop=True), "T")
        t_onb = tv
        w("vector", "T", t_onb)
        for mt in range(NMT):
            r0 = c * CH + mt * WIN
            layer_stats(ACC[:, mt * D:(mt + 1) * D], D)
            E("vector", lambda e, m=mt: e.tensor_scalar(
                FN, ACC[:, m * D:(m + 1) * D], st_rstd, st_negmur,
                op0=ALU.mult, op1=ALU.add))
            E("vector", lambda e: e.tensor_mul(FN, FN, PA))
            E("vector", lambda e: e.tensor_add(FN, FN, PB))
            if c > 0 or mt > 0:
                wdma_all("vector", "st")
            E("vector", lambda e, m=mt: e.tensor_scalar(
                OUT, H[:, m * D:(m + 1) * D],
                MUSIG[:, 2 * m + 1:2 * m + 2], MUSIG[:, 2 * m:2 * m + 1],
                op0=ALU.mult, op1=ALU.add))
            vv = E("vector", lambda e: e.tensor_add(OUT, OUT, FN), "V")
            w("gpsimd", "V", vv)
            dma_group("st", [(y_d[r0:r0 + 128, :], OUT)])
        rel[("onfree", c)] = vv

    wdma_all("gpsimd", "st")
    wdma_all("gpsimd", "x0")
    wdma_all("sync", "ld")

    with nc.Block() as block:
        @block.sync
        def _(e):
            for fn in q["sync"]:
                fn(e)

        @block.tensor
        def _(e):
            for fn in q["tensor"]:
                fn(e)

        @block.vector
        def _(e):
            for fn in q["vector"]:
                fn(e)

        @block.scalar
        def _(e):
            for fn in q["scalar"]:
                fn(e)

        @block.gpsimd
        def _(e):
            for fn in q["gpsimd"]:
                fn(e)

    stack.close()
    return nc


def host_inputs(inp):
    f32 = np.float32
    asc = np.ascontiguousarray

    def t(a):
        return asc(np.asarray(a, dtype=f32).T)

    def pp(a):
        return asc(np.asarray(a, dtype=f32).reshape(8, 128).T)

    def pp16(a):
        return asc(np.asarray(a, dtype=f32).reshape(4, 16, 128).transpose(2, 0, 1)
                   .reshape(128, 64))

    def row(a):
        return asc(np.asarray(a, dtype=f32).reshape(1, -1))

    shared = {
        "wqT": t(inp["L0_attn_Wq"]),
        "wkT": t(inp["L0_attn_Wk"]),
        "wvT": t(inp["L0_attn_Wv"]),
        "bq": pp(inp["L0_attn_bq"]),
        "bk": pp(inp["L0_attn_bk"]),
        "bv": row(inp["L0_attn_bv"]),
        "gam": row(inp["L0_attn_gamma"]),
        "bet": row(inp["L0_attn_beta"]),
        "rwT": t(inp["L0_ffn_router_w"]),
        "rng": pp(inp["L0_ffn_rn_g"]),
        "rnb": pp(inp["L0_ffn_rn_b"]),
        "w1": asc(np.asarray(inp["L0_ffn_w1"], dtype=f32)),
        "lng": pp16(inp["L0_ffn_ln_g"]),
        "lnb": pp16(inp["L0_ffn_ln_b"]),
        "w2": asc(np.asarray(inp["L0_ffn_w2"], dtype=f32)),
        "ong": row(inp["L0_ffn_on_g"]),
        "onb": row(inp["L0_ffn_on_b"]),
        "ident": np.eye(128, dtype=f32),
    }
    return shared


_NC_CACHE = {}


def kernel(**inputs):
    full_x = np.asarray(inputs["L0_attn_x"], dtype=np.float32)
    shared = host_inputs(inputs)
    if "nc" not in _NC_CACHE:
        _NC_CACHE["nc"] = build_nc(n_chunks=8)
    nc = _NC_CACHE["nc"]
    from concourse.bass_utils import run_bass_kernel_spmd
    in_maps = []
    for b in range(8):
        m = dict(shared)
        m["x"] = np.ascontiguousarray(full_x[b])
        in_maps.append(m)
    res = run_bass_kernel_spmd(nc, in_maps, core_ids=list(range(8)))
    out = np.stack([np.asarray(res.results[b]["y"], dtype=np.float32)
                    for b in range(8)], axis=0)
    return out
